# version 04
# baseline (speedup 1.0000x reference)
import numpy as np
import ml_dtypes
import concourse.bass as bass
import concourse.mybir as mybir
from concourse.bass import IndirectOffsetOnAxis
from concourse.bass_utils import run_bass_kernel_spmd

F32 = mybir.dt.float32
F16 = mybir.dt.float16
BF16 = mybir.dt.bfloat16
U32 = mybir.dt.uint32
AF = mybir.ActivationFunctionType
ALU = mybir.AluOpType
AXX = mybir.AxisListType.X

D = 1024
S = 2048
NB = 32
NCORE = 8
BPC = 4
NT = 16
NTC = BPC * NT
E = 64
CAP = 512
ISH = 2048
EPS = 1e-5
EBW = 3968
HHW = 4095
BIGOFF = float(1 << 20)

PHASES = "full"
STOP_AFTER = None
VDBG = 0
DUMP = None
PART_MODE = "split"
DBG = False

ENGINES = ("sync", "tensor", "vector", "scalar", "gpsimd")


def host_consts(core_id):
    ident_bf = np.eye(128, dtype=ml_dtypes.bfloat16)
    ident_f = np.eye(128, dtype=np.float32)
    tri_bf = np.triu(np.ones((128, 128), dtype=np.float32), 1).astype(ml_dtypes.bfloat16)
    iota = np.tile(np.arange(64, dtype=np.float32)[None, :], (128, 1))
    ones_bf = np.ones((1, 128), dtype=ml_dtypes.bfloat16)
    ones_f = np.ones((1, 128), dtype=np.float32)
    stairidx = np.arange(128, dtype=np.uint32).reshape(128, 1)
    u = np.arange(HHW)
    jidx = np.clip(2047 - u, -128, 128) + 128
    G = np.zeros((384, HHW), dtype=np.float32)
    G[jidx, u] = 1.0
    G3 = np.ascontiguousarray(G.reshape(3, 128, HHW))
    tb = np.zeros((32, 4), dtype=np.float32)
    for r in range(8):
        for i2 in range(4):
            grow = r + 8 * i2
            for i in range(4):
                if grow < core_id + 8 * i:
                    tb[r * 4 + i2, i] = 1.0
    return {
        "c_ident_bf": ident_bf, "c_ident_f": ident_f, "c_tri_bf": tri_bf,
        "c_iota": iota, "c_ones_bf": ones_bf, "c_ones_f": ones_f,
        "c_stairidx": stairidx, "c_G": G3, "c_tribase": tb,
        "c_ones_col": np.ones((128, 1), dtype=ml_dtypes.bfloat16),
    }


ALIAS = {}
for grp in [("p0c", "p0d", "p0e", "p0f"), ("b0a", "b0b", "sB0", "sB1"),
            ("b5sq", "b5xc", "b5vr", "b5sd"), ("aXC", "aSQ", "aVR", "aSD"),
            ("aRT1", "aRT2"), ("b1rb", "b1po", "b1bc", "sTBr"),
            ("aRC", "aBC", "aEVB"), ("sFIN", "sOUT"),
            ("sGS", "sHH"), ("p0z", "p0v"), ("b5gz", "b5g"),
            ("aCNT", "aRUN"), ("aRT3", "aRUN"), ("sCI", "sX1"), ("b0x", "sB2l"),
            ("a5sg", "a5gu"), ("b4sg", "b4gu"), ("aQM", "aSC0"), ("aKM", "aSC0"),
            ("aVM", "aSC0")]:
    for n in grp[:-1]:
        ALIAS[n] = grp[-1]


class Walk:
    def __init__(self, nc):
        self.nc = nc
        self.handles = {}
        self.counts = {}
        self.cur = None
        self.eng = None

    def h(self, name):
        name = ALIAS.get(name, name)
        if name not in self.handles:
            self.handles[name] = self.nc.alloc_semaphore(name)
        return self.handles[name]

    def begin(self, engine_name, eng):
        self.cur = engine_name
        self.eng = eng
        self.counts = {}

    def c(self, name):
        return self.counts.get(ALIAS.get(name, name), 0)

    def st(self, engine, fn=None, waits=(), incs=(), ss=False):
        if ss and not incs:
            incs = [("vch_" + engine, 1)]
        if engine == self.cur:
            for nm, val in waits:
                if val > 0:
                    self.eng.wait_ge(self.h(nm), val)
            ins = fn(self.eng) if fn is not None else None
            if ins is None and incs:
                ins = self.eng.nop()
            for nm, n in incs:
                ins = ins.then_inc(self.h(nm), n)
        for nm, n in incs:
            nm = ALIAS.get(nm, nm)
            self.counts[nm] = self.counts.get(nm, 0) + n
        if ss:
            nm = ALIAS.get(incs[0][0], incs[0][0])
            if engine == self.cur:
                self.eng.wait_ge(self.h(nm), self.counts.get(nm, 0))

    def barrier(self, tag):
        nm = "bar_" + tag
        for e in ENGINES:
            self.st(e, incs=[(nm, 1)])
        tgt = self.c(nm)
        for e in ENGINES:
            self.st(e, waits=[(nm, tgt)])


def build_nc(part="full", dbg=False):
    nc = bass.Bass("TRN2", target_bir_lowering=False)

    def din(name, shape, dtype):
        return nc.dram_tensor(name, list(shape), dtype, kind="ExternalInput").ap()

    def dint(name, shape, dtype, out=False):
        return nc.dram_tensor(name, list(shape), dtype,
                              kind="ExternalOutput" if out else "Internal").ap()

    isp1 = part == "p1"
    isp2 = part == "p2"
    x4 = din("x4", [BPC, S, D], F32) if not isp2 else None
    if not isp2:
        Wq = din("Wq", [D, D], F32); bq = din("bq", [D], F32)
        Wk = din("Wk", [D, D], F32); bk = din("bk", [D], F32)
        Wv = din("Wv", [D, D], F32); bv = din("bv", [D], F32)
        Wo = din("Wo", [D, D], F32); bo = din("bo", [D], F32)
        relb = din("relb", [16, 257], F32)
        gate_w = din("gate_w", [D, E], F32)
        sgw = din("sgw", [D, 1], F32)
        wsgu = din("wsgu", [D, 2 * ISH], F32)
        sdw = din("sdw", [ISH, D], F32)
    if not isp1:
        egu = din("egu", [E, D, D], F32)
        edw = din("edw", [E, CAP, D], F32)
    c_ident_bf = din("c_ident_bf", [128, 128], BF16)
    c_iota = din("c_iota", [128, 64], F32)
    c_ones_f = din("c_ones_f", [1, 128], F32)
    if not isp2:
        c_ident_f = din("c_ident_f", [128, 128], F32)
        c_tri_bf = din("c_tri_bf", [128, 128], BF16)
        c_ones_bf = din("c_ones_bf", [1, 128], BF16)
        c_stairidx = din("c_stairidx", [128, 1], U32)
        c_G = din("c_G", [3, 128, HHW], F32)
        c_ones_col = din("c_ones_col", [128, 1], BF16)
    if not isp1:
        c_tribase = din("c_tribase", [32, 4], F32)

    p1 = part == "p1"
    p2 = part == "p2"
    out4 = dint("out4", [BPC, S, D], F32, out=True) if not p1 else None

    d_wT = {w: dint(f"d_w{w}T", [8, 8, 128, 128], BF16) for w in "qkv"}
    d_hh = dint("d_hh", [16, HHW], F32)
    if p2:
        d_x1f = din("d_x1f", [BPC, S, D], F32)
        d_x1b = din("d_x1b", [BPC, S, D], BF16)
        d_shared = din("d_shared", [BPC, S, D], F32)
        d_x1T = None
    else:
        d_x1f = dint("d_x1f", [BPC, S, D], F32, out=(dbg or p1))
        d_x1b = dint("d_x1b", [BPC, S, D], BF16, out=p1)
        d_x1T = dint("d_x1T", [BPC, 8, 128, S], BF16)
        d_shared = dint("d_shared", [BPC, S, D], F32, out=(dbg or p1))
    d_buf = dint("d_buf", [E * CAP, D], BF16)
    d_obuf = dint("d_obuf", [E * CAP, D], F32)
    d_cin = dint("d_cin", [BPC, E], F32, out=p1) if not p2 else None
    d_cout = din("d_cout", [NB, E], F32) if p2 else dint("d_cout", [NB, E], F32)
    d_base = dint("d_base", [BPC, E], F32) if not p1 else None
    d_tb = din("d_tb", [NTC, E], F32) if p2 else dint("d_tb", [NTC, E], F32, out=p1)
    if p1:
        d_idxf = dint("d_idxf", [128, NTC * 8], F32, out=True)
        d_wv = dint("d_wv", [128, NTC * 8], F32, out=True)
        d_po = dint("d_po", [128, NTC * 64], F16, out=True)
    elif p2:
        d_idxf = din("d_idxf", [128, NTC * 8], F32)
        d_wv = din("d_wv", [128, NTC * 8], F32)
        d_po = din("d_po", [128, NTC * 64], F16)

    from contextlib import ExitStack
    stack = ExitStack()
    ent = stack.enter_context

    def al(name, shape, dtype):
        return ent(nc.sbuf_tensor(name, list(shape), dtype))

    ident_bf = al("ident_bf", [128, 128], BF16)
    tri_bf = al("tri_bf", [128, 128], BF16)
    iota = al("iota", [128, 64], F32)
    ones_bf = al("ones_bf", [1, 128], BF16)
    ones_f = al("ones_f", [1, 128], F32)
    ones_col = al("ones_col", [128, 1], BF16)
    stairidx = al("stairidx", [128, 1], U32)
    bqs = al("bqs", [128, 8], F32)
    bks = al("bks", [128, 8], F32)
    bvrow = al("bvrow", [1, D], BF16)
    borow = al("borow", [1, D], F32)
    gcat = al("gcat", [128, 8, 65], BF16)
    idxf_all = al("idxf_all", [128, NTC * 8], F32)
    w_all = al("w_all", [128, NTC * 8], F32)
    wm_all = al("wm_all", [128, NTC * 8], F32)
    o8u_all = al("o8u_all", [128, NTC * 8], U32)
    po_all = al("po_all", [128, NTC * 64], F16)
    g_all = al("g_all", [128, NTC], F32)
    run2 = al("run2", [1, 128], F32)
    cnt_sb = al("cnt_sb", [1, BPC * 64], F32)
    base_row = al("base_row", [1, BPC * 64], F32)
    zero_bf = al("zero_bf", [128, 1024], BF16)

    AREA = 178 * 1024
    arena = al("arena", [128, AREA], mybir.dt.uint8)

    class Carver:
        def __init__(self):
            self.off = 0

        def take(self, nbytes, shape, dtype):
            esz = {F32: 4, BF16: 2, F16: 2, U32: 4}[dtype]
            off = (self.off + 127) & ~127
            self.off = off + nbytes
            assert self.off <= AREA, f"arena overflow {self.off}"
            n = 1
            for s_ in shape[1:]:
                n *= s_
            assert n * esz <= nbytes, (shape, nbytes)
            flat = arena.bitcast(dtype)[0:shape[0], off // esz: off // esz + n]
            if len(shape) == 2:
                return flat
            return flat.rearrange("p (a b) -> p a b", a=shape[1], b=shape[2])

    cv = Carver()
    xbT = cv.take(32768, [128, 8, S], BF16)
    attn = cv.take(32768, [128, 8, S], BF16)
    qpair = cv.take(4096, [128, S], BF16)
    kpair = cv.take(4096, [128, S], BF16)
    vpair = cv.take(4352, [128, NT * 2, 66], BF16)
    wsl = [[cv.take(2048, [128, 8, 128], BF16) for _ in range(3)] for _ in range(2)]
    wobuf = cv.take(16384, [128, 8, D], BF16)
    ebigs = [cv.take(8064, [128, EBW], BF16) for _ in range(4)]
    exp_sb = [cv.take(1024, [128, 512], BF16) for _ in range(2)]
    ebm_sb = [cv.take(1024, [128, 512], BF16) for _ in range(2)]
    rz = cv.take(2048, [1, 512], F32)
    b_sb = cv.take(2048, [64, 512], F32)
    xb16 = [cv.take(2048, [128, 1024], BF16) for _ in range(2)]
    xres = [cv.take(4096, [128, 1024], F32) for _ in range(2)]
    s_sb = cv.take(4096, [128, 1024], F32)
    xc_sb = cv.take(4096, [128, 1024], F32)
    x1f_sb = cv.take(4096, [128, 1024], F32)
    x1b_sb = cv.take(2048, [128, 1024], BF16)
    x1T_sb = cv.take(2048, [128, 8, 128], BF16)
    prob_sb = cv.take(256, [128, 64], F32)
    O_bf = cv.take(128, [128, 64], BF16)
    w8raw = cv.take(32, [128, 8], F32)
    colA = cv.take(128, [128, 8], F32)

    cv5 = Carver()
    wsgu_sb = cv5.take(65536, [128, 8, 2 * ISH], BF16)
    sdw_sb = cv5.take(32768, [128, 16, D], BF16)
    x1c = [cv5.take(8192, [128, 8, 512], BF16) for _ in range(2)]
    relT = cv5.take(256, [128, 3 * 16], F32)
    gst = cv5.take(6144, [128, 3, 512], F32)
    hh_sb = cv5.take(16384, [16, HHW], F32)
    identf_sb = cv5.take(512, [128, 128], F32)
    act5 = cv5.take(16384, [128, 16 * 512], BF16)
    sg5 = cv5.take(2048, [128, 512], F32)
    sh5 = cv5.take(4096, [128, 1024], F32)

    cvB = Carver()
    wgu = [cvB.take(16384, [128, 8, D], BF16) for _ in range(2)]
    wdw = [cvB.take(8192, [128, 4, D], BF16) for _ in range(2)]
    rows = [cvB.take(8192, [128, 4, D], BF16) for _ in range(2)]
    rowsT = [cvB.take(8192, [128, 8 * 512], BF16) for _ in range(2)]
    acteB = cvB.take(4096, [128, 4 * 512], BF16)
    sgeB = cvB.take(2048, [128, 512], F32)
    outeB = cvB.take(4096, [128, 1024], F32)
    x1s = [cvB.take(2048, [128, 1024], BF16) for _ in range(2)]
    G8 = cvB.take(32768, [128, 8, 1024], F32)
    xrB = [cvB.take(4096, [128, 1024], F32) for _ in range(2)]
    shB = [cvB.take(4096, [128, 1024], F32) for _ in range(2)]
    POg = cvB.take(256, [128, 64], F32)
    selB = cvB.take(256, [128, 64], F32)
    posg8 = cvB.take(32, [128, 8], F32)
    o8f = cvB.take(32, [128, 8], F32)
    m8 = cvB.take(32, [128, 8], F32)
    rbB = cvB.take(256, [1, 64], F32)
    sB_ln = cvB.take(4096, [128, 1024], F32)
    xcB = cvB.take(4096, [128, 1024], F32)
    accB = [cvB.take(4096, [128, 1024], F32) for _ in range(2)]
    colB = cvB.take(128, [128, 8], F32)
    ccsb = cvB.take(256, [32, 64], F32)
    tb32 = cvB.take(128, [32, 4], F32)
    bases4 = cvB.take(256, [4, 64], F32)

    ps = [ent(nc.psum_tensor(f"ps{i}", [128, 512], F32)) for i in range(6)]
    pst = [ent(nc.psum_tensor(f"pst{i}", [128, 1024], BF16)) for i in range(2)]

    blk = ent(nc.Block())
    w = Walk(nc)
    rg = [list(range(NCORE))]

    breg = {}

    def program():
        st = w.st

        def mkbnd(e):
            breg["r"] = e.alloc_register("bnd")
            return e.reg_mov(breg["r"], E * CAP - 1)
        if not p1:
            st("gpsimd", mkbnd)

        p0_list = [(ident_bf, c_ident_bf), (iota, c_iota), (ones_f, c_ones_f)]
        if not p2:
            p0_list += [(tri_bf, c_tri_bf), (ones_bf, c_ones_bf),
                        (stairidx, c_stairidx), (identf_sb, c_ident_f),
                        (ones_col, c_ones_col)]
        for sbuf, src in p0_list:
            st("sync", lambda e, o=sbuf, i=src: e.dma_start(out=o[:], in_=i[:]),
               incs=[("p0l", 16)])
        if p2:
            st("sync", lambda e: e.dma_start(out=idxf_all[:], in_=d_idxf[:]), incs=[("pRT", 16)])
            st("sync", lambda e: e.dma_start(out=w_all[:], in_=d_wv[:]), incs=[("pRT", 16)])
            st("sync", lambda e: e.dma_start(out=po_all[:], in_=d_po[:]), incs=[("pRT", 16)])
        if not p2:
            for c in range(8):
                st("sync", lambda e, c=c: e.dma_start(
                    out=bqs[:, c:c + 1], in_=bq[c * 128:(c + 1) * 128].unsqueeze(1)),
                   incs=[("p0l", 16)])
                st("sync", lambda e, c=c: e.dma_start(
                    out=bks[:, c:c + 1], in_=bk[c * 128:(c + 1) * 128].unsqueeze(1)),
                   incs=[("p0l", 16)])
            st("gpsimd", lambda e: e.dma_start(out=bvrow[:], in_=bv.unsqueeze(0)),
               incs=[("p0l", 16)])
            st("sync", lambda e: e.dma_start(out=borow[:], in_=bo.unsqueeze(0)),
               incs=[("p0l", 16)])
            st("gpsimd", lambda e: e.dma_start(
                out=gcat[:, :, 0:64], in_=gate_w.rearrange("(c p) f -> p c f", p=128)),
               incs=[("p0l", 16)])
            for c in range(8):
                st("gpsimd", lambda e, c=c: e.dma_start(
                    out=gcat[:, c, 64:65], in_=sgw[c * 128:(c + 1) * 128, :]),
                   incs=[("p0l", 16)])
            st("sync", lambda e: e.dma_start(out=hh_sb[0:16, 0:257], in_=relb[:]),
               incs=[("p0l", 16)])
            st("sync", lambda e: e.dma_start(
                out=relT[0:1, 32:48], in_=relb[:, 256:257].transpose([1, 0]),
                allow_slow_non_contiguous=True),
               incs=[("p0l", 16)])

        if not p1:
            st("vector", lambda e: e.memset(zero_bf[:], 0.0), incs=[("p0z", 1)])
            for n in range(256):
                st("scalar", lambda e, n=n: e.dma_start(
                    out=d_buf[n * 128:(n + 1) * 128, :], in_=zero_bf[:]),
                   waits=[("p0z", 1)] if n == 0 else [], incs=[("sZ", 16)])

        p0l_tot = w.c("p0l")
        for ch in ([] if p2 else range(2)):
            st("tensor", lambda e, ch=ch: e.transpose(
                ps[2][:, 0:16], hh_sb[0:16, ch * 128:(ch + 1) * 128], identf_sb[0:16, 0:16]),
               waits=[("p0l", p0l_tot), ("p0d", w.c("p0d"))], incs=[("p0c", 1)])
            st("vector", lambda e, ch=ch: e.tensor_copy(
                relT[:, ch * 16:(ch + 1) * 16], ps[2][:, 0:16]),
               waits=[("p0c", w.c("p0c"))], incs=[("p0d", 1)])
        for oc in ([] if p2 else range(8)):
            ncol = 512 if oc < 7 else HHW - 7 * 512
            for ch in range(3):
                st("sync", lambda e, oc=oc, ch=ch, ncol=ncol: e.dma_start(
                    out=gst[:, ch, 0:ncol], in_=c_G[ch, :, oc * 512:oc * 512 + ncol]),
                   waits=[("p0e", w.c("p0e"))] if ch == 0 else [], incs=[("sGS", 16)])
            def mmh(e, oc=oc, ncol=ncol):
                ins = None
                for ch in range(3):
                    kk = 128 if ch < 2 else 1
                    ins = e.matmul(ps[3][0:16, 0:ncol], relT[0:kk, ch * 16:(ch + 1) * 16],
                                   gst[0:kk, ch, 0:ncol],
                                   start=(ch == 0), stop=(ch == 2))
                return ins
            st("tensor", mmh, waits=[("p0d", w.c("p0d")), ("p0f", w.c("p0f")), ("sGS", w.c("sGS"))],
               incs=[("p0e", 1)])
            st("vector", lambda e, oc=oc, ncol=ncol: e.tensor_copy(
                hh_sb[0:16, oc * 512:oc * 512 + ncol], ps[3][0:16, 0:ncol]),
               waits=[("p0e", w.c("p0e"))], incs=[("p0f", 1)])
        if not p2:
            st("sync", lambda e: e.dma_start(out=d_hh[:], in_=hh_sb[0:16, :]),
               waits=[("p0f", w.c("p0f"))], incs=[("sHH", 16)])

        for wname, wsrc in ([] if p2 else [("q", Wq), ("k", Wk), ("v", Wv), ("o", Wo)]):
            for r in range(8):
                st("gpsimd", lambda e, wsrc=wsrc, r=r: e.dma_start(
                    out=xb16[r % 2][:], in_=wsrc[r * 128:(r + 1) * 128, :]),
                   waits=[("p0t", max(0, w.c("p0t") - 1))], incs=[("p0w", 16)])
                def trs(e, r=r):
                    ins = None
                    for c in range(8):
                        ins = e.transpose(pst[r % 2][:, c * 128:(c + 1) * 128],
                                          xb16[r % 2][:, c * 128:(c + 1) * 128], ident_bf[:])
                    return ins
                st("tensor", trs,
                   waits=[("p0w", w.c("p0w")), ("p0g", max(0, w.c("p0g") - 1))],
                   incs=[("p0t", 1)])
                st("vector", lambda e, r=r: e.tensor_copy(
                    wobuf[:, :, r * 128:(r + 1) * 128],
                    pst[r % 2].rearrange("p (c t) -> p c t", c=8)),
                   waits=[("p0t", w.c("p0t"))] + ([("sWT", w.c("sWT"))] if r == 0 else []),
                   incs=[("p0g", 1)])
            if wname != "o":
                for hp in range(8):
                    st("sync", lambda e, wname=wname, hp=hp: e.dma_start(
                        out=d_wT[wname][hp].rearrange("c p t -> p c t"),
                        in_=wobuf[:, :, hp * 128:(hp + 1) * 128]),
                       waits=[("p0g", w.c("p0g"))], incs=[("sWT", 16)])

        if not p2:
            st("vector", lambda e: e.memset(vpair[:, :, 64:65], 1.0), incs=[("p0v", 1)])

        st("sync", waits=[("sHH", w.c("sHH")), ("sWT", w.c("sWT")),
                          ("p0l", w.c("p0l")), ("pRT", w.c("pRT"))])
        w.barrier("p0")
        if STOP_AFTER == "p0":
            return

        for bi in ([] if p2 else range(BPC)):
            a_xb = w.c("sXB"); a_tr = w.c("aTR"); a_ev = w.c("aEV")
            axte0 = w.c("aXTE")
            for tt in range(NT):
                st("gpsimd", lambda e, bi=bi, tt=tt: e.dma_start(
                    out=xb16[tt % 2][:], in_=x4[bi, tt * 128:(tt + 1) * 128, :]),
                   waits=[("aTR", a_tr + tt - 1)], incs=[("sXB", 16)])
                def trx(e, tt=tt):
                    ins = None
                    for c in range(8):
                        ins = e.transpose(pst[tt % 2][:, c * 128:(c + 1) * 128],
                                          xb16[tt % 2][:, c * 128:(c + 1) * 128], ident_bf[:])
                    return ins
                st("tensor", trx,
                   waits=[("sXB", a_xb + 16 * (tt + 1)), ("aEV", a_ev + tt - 1)]
                         + ([("aXTE", axte0)] if tt <= 1 else []),
                   incs=[("aTR", 1)])
                st("vector", lambda e, tt=tt: e.tensor_copy(
                    xbT[:, :, tt * 128:(tt + 1) * 128],
                    pst[tt % 2].rearrange("p (c t) -> p c t", c=8)),
                   waits=[("aTR", a_tr + tt + 1)], incs=[("aEV", 1)])
            if STOP_AFTER == "a1":
                st("vector", waits=[("aEV", w.c("aEV"))])
                return

            for hp in range(8):
                lw = wsl[hp % 2]
                for k3, wname in enumerate("qkv"):
                    st("sync", lambda e, wname=wname, hp=hp, k3=k3, lw=lw: e.dma_start(
                        out=lw[k3][:], in_=d_wT[wname][hp].rearrange("c p t -> p c t")),
                       waits=[("sWT", w.c("sWT")), ("aVM", max(0, w.c("aVM") - NT))],
                       incs=[("sWS", 16)])
                if STOP_AFTER == "wsl" and hp == 0 and bi == 0:
                    st("sync", waits=[("sWS", w.c("sWS"))])
                    return
                for hh in range(2):
                    h = hp * 2 + hh
                    ebuf = ebigs[h % 4]
                    st("gpsimd", lambda e, h=h, ebuf=ebuf: e.indirect_dma_start(
                        out=ebuf[:, 0:1984], out_offset=None,
                        in_=d_hh.flatten().unsqueeze(1),
                        in_offset=IndirectOffsetOnAxis(stairidx[:], 0),
                        element_offset=h * HHW),
                       waits=[("sHH", w.c("sHH")), ("aEM", max(0, w.c("aEM") - 128))],
                       incs=[("sEBG", 16)])
                    st("gpsimd", lambda e, h=h, ebuf=ebuf: e.indirect_dma_start(
                        out=ebuf[:, 1984:EBW], out_offset=None,
                        in_=d_hh.flatten().unsqueeze(1),
                        in_offset=IndirectOffsetOnAxis(stairidx[:], 0),
                        element_offset=h * HHW + 1984),
                       incs=[("sEBG", 16)])
                    st("scalar", lambda e, ebuf=ebuf: e.activation(ebuf[:], ebuf[:], AF.Exp),
                       waits=[("sEBG", w.c("sEBG"))], incs=[("aEB", 1)])

                if STOP_AFTER == "eb" and hp == 0 and bi == 0:
                    st("scalar", waits=[("aEB", w.c("aEB"))])
                    return
                a_ws = w.c("sWS")
                for qc in range(4):
                    def mmq(e, qc=qc, lw=lw):
                        ins = None
                        for c in range(8):
                            ins = e.matmul(ps[2 + qc % 2][:], lw[0][:, c, :],
                                           xbT[:, c, qc * 512:(qc + 1) * 512],
                                           start=(c == 0), stop=(c == 7))
                        return ins
                    st("tensor", mmq,
                       waits=[("sWS", a_ws), ("aEV", a_ev + 16),
                              ("aPJq", max(0, w.c("aPJq") - 1))]
                             + ([("aEVB", w.c("aEVB")), ("aFIN", w.c("aFIN"))] if qc == 0 else []),
                       incs=[("aQM", 1)])
                    st("vector", lambda e, qc=qc, hp=hp: e.tensor_scalar_add(
                        qpair[:, (3 - qc) * 512:(4 - qc) * 512][:, ::-1],
                        ps[2 + qc % 2][:], bqs[:, hp:hp + 1]),
                       waits=[("aQM", w.c("aQM"))], incs=[("aPJq", 1)])
                aq_end = w.c("aPJq")
                if STOP_AFTER == "q" and hp == 0 and bi == 0:
                    st("vector", waits=[("aPJq", aq_end)])
                    return
                for qc in range(4):
                    def mmk(e, qc=qc, lw=lw):
                        ins = None
                        for c in range(8):
                            ins = e.matmul(ps[2 + qc % 2][:], lw[1][:, c, :],
                                           xbT[:, c, qc * 512:(qc + 1) * 512],
                                           start=(c == 0), stop=(c == 7))
                        return ins
                    st("tensor", mmk,
                       waits=([("aPJq", aq_end)] if qc < 2 else [])
                             + [("aPJk", max(0, w.c("aPJk") - 1))],
                       incs=[("aKM", 1)])
                    st("vector", lambda e, qc=qc, hp=hp: e.tensor_scalar_add(
                        kpair[:, qc * 512:(qc + 1) * 512], ps[2 + qc % 2][:], bks[:, hp:hp + 1]),
                       waits=[("aKM", w.c("aKM"))], incs=[("aPJk", 1)])
                ak_end = w.c("aPJk")
                if STOP_AFTER == "k" and hp == 0 and bi == 0:
                    st("vector", waits=[("aPJk", ak_end)])
                    return
                for tt in range(NT):
                    def mmv(e, tt=tt, hp=hp, lw=lw):
                        ins = None
                        nmm = 8 if VDBG != 1 else 7
                        for c in range(nmm):
                            ins = e.matmul(ps[2 + tt % 2][:, 0:128],
                                           xbT[:, c, tt * 128:(tt + 1) * 128],
                                           lw[2][:, c, :], start=(c == 0),
                                           stop=(VDBG == 1 and c == nmm - 1))
                        if VDBG != 1:
                            ins = e.matmul(ps[2 + tt % 2][:, 0:128], ones_bf[:],
                                           bvrow[0:1, hp * 128:(hp + 1) * 128],
                                           start=False, stop=True)
                        return ins
                    st("tensor", mmv,
                       waits=([("aPJk", ak_end)] if tt < 2 else [])
                             + [("aPJv", max(0, w.c("aPJv") - 1))],
                       incs=[("aVM", 1)])
                    if VDBG == 3:
                        st("vector", lambda e: e.nop(), waits=[("aVM", w.c("aVM"))], incs=[("aPJv", 1)])
                    elif VDBG == 2:
                        st("vector", lambda e, tt=tt: e.tensor_copy(
                            vpair[:, 2 * tt, 0:64],
                            ps[2 + tt % 2][:, 0:64]),
                           waits=[("aVM", w.c("aVM")), ("p0v", w.c("p0v"))], incs=[("aPJv", 1)])
                    else:
                        st("vector", lambda e, tt=tt: e.tensor_copy(
                            vpair[:, 2 * tt:2 * tt + 2, 0:64],
                            ps[2 + tt % 2][:, 0:128].rearrange("p (s c) -> p s c", c=64)),
                           waits=[("aVM", w.c("aVM")), ("p0v", w.c("p0v"))], incs=[("aPJv", 1)])
                av_end = w.c("aPJv")
                if STOP_AFTER == "proj":
                    st("vector", waits=[("aPJv", av_end)])
                    return

                for hh in range(2):
                    ebuf = ebigs[(hp * 2 + hh) % 4]
                    a_eb = w.c("aEB")
                    b_sc = w.c("aSC"); b_ex = w.c("aEX"); b_em = w.c("aEM")
                    b_av = w.c("aAV"); b_fin = w.c("aFIN")
                    NU = 64
                    for u in range(NU + 1):
                        if u < NU:
                            qc, kt = divmod(u, 16)
                            st("tensor", lambda e, qc=qc, kt=kt, hh=hh, u=u: e.matmul(
                                ps[u % 2][:],
                                kpair[hh * 64:hh * 64 + 64, kt * 128:(kt + 1) * 128],
                                qpair[hh * 64:hh * 64 + 64, qc * 512:(qc + 1) * 512],
                                start=True, stop=True),
                               waits=[("aEX", b_ex + u - 1)]
                                     + ([("aPJv", av_end)] if u == 0 else []),
                               incs=[("aSC", 1)])
                        if u >= 1:
                            v = u - 1
                            qc, kt = divmod(v, 16)
                            st("scalar", lambda e, v=v: e.activation(
                                exp_sb[v % 2][:], ps[v % 2][:], AF.Exp, scale=0.125),
                               waits=[("aSC", b_sc + v + 1), ("aEM", b_em + v - 1),
                                      ("aEB", a_eb)],
                               incs=[("aEX", 1)])
                            st("vector", lambda e, v=v, qc=qc, kt=kt, ebuf=ebuf: e.tensor_tensor(
                                ebm_sb[v % 2][:], exp_sb[v % 2][:],
                                ebuf[:, qc * 512 + kt * 128:qc * 512 + kt * 128 + 512],
                                ALU.mult),
                               waits=[("aEX", b_ex + v + 1), ("aAV", b_av + v - 1)],
                               incs=[("aEM", 1)])
                            st("tensor", lambda e, v=v, qc=qc, kt=kt, hh=hh: e.matmul(
                                ps[4 + qc % 2][0:65, :],
                                vpair[:, 2 * kt + hh, 0:65],
                                ebm_sb[v % 2][:], start=(kt == 0), stop=(kt == 15)),
                               waits=[("aEM", b_em + v + 1), ("aFIN", b_fin + qc - 1)],
                               incs=[("aAV", 1)])
                            if kt == 15:
                                st("vector", lambda e, qc=qc: e.reciprocal(
                                    rz[:], ps[4 + qc % 2][64:65, :]),
                                   waits=[("aAV", b_av + v + 1), ("aBC", w.c("aBC"))],
                                   incs=[("aRC", 1)])
                                st("tensor", lambda e: e.matmul(
                                    ps[2][0:64, :], ones_f[0:1, 0:64], rz[:],
                                    start=True, stop=True),
                                   waits=[("aRC", w.c("aRC")), ("aEVB", w.c("aEVB"))],
                                   incs=[("aBC", 1)])
                                st("vector", lambda e: e.tensor_copy(b_sb[:], ps[2][0:64, :]),
                                   waits=[("aBC", w.c("aBC"))], incs=[("aEVB", 1)], ss=True)
                                st("vector", lambda e, qc=qc, hp=hp, hh=hh: e.tensor_tensor(
                                    attn[hh * 64:hh * 64 + 64, hp,
                                         (3 - qc) * 512:(4 - qc) * 512][:, ::-1],
                                    ps[4 + qc % 2][0:64, :], b_sb[:], ALU.mult),
                                   waits=[("aEVB", w.c("aEVB"))], incs=[("aFIN", 1)])

            if STOP_AFTER == "att1" and bi == 0:
                st("vector", waits=[("aFIN", w.c("aFIN"))])
                return
            if DUMP == "attn":
                st("sync", lambda e, bi=bi: e.dma_start(
                    out=d_x1b[bi].rearrange("t d -> (t d)").rearrange("(p f) -> p f", p=128),
                    in_=attn.rearrange("p a b -> p (a b)")),
                   waits=[("aFIN", w.c("aFIN"))], incs=[("sX1", 16)])
            st("vector", lambda e: e.memset(run2[0:1, 0:64], 0.0), incs=[("aRUN", 1)])
            a_wo = w.c("aWO"); a_res = w.c("aRES"); a_x = w.c("sXR")
            afin = w.c("aFIN")
            for tt in range(NT):
                gt = bi * NT + tt
                st("sync", lambda e, bi=bi, tt=tt: e.dma_start(
                    out=xres[tt % 2][:], in_=x4[bi, tt * 128:(tt + 1) * 128, :]),
                   waits=[("aRES", a_res + 2 * (tt - 1))], incs=[("sXR", 16)])
                for half in range(2):
                    def mmo(e, tt=tt, half=half):
                        ins = None
                        for c in range(8):
                            ins = e.matmul(ps[half][:],
                                           attn[:, c, tt * 128:(tt + 1) * 128],
                                           wobuf[:, c, half * 512:(half + 1) * 512],
                                           start=(c == 0), stop=False)
                        ins = e.matmul(ps[half][:], ones_f[:],
                                       borow[0:1, half * 512:(half + 1) * 512],
                                       start=False, stop=True)
                        return ins
                    st("tensor", mmo,
                       waits=([("aFIN", afin), ("aEX", w.c("aEX")), ("sWT", w.c("sWT"))]
                              if tt == 0 and half == 0 else [])
                             + [("aRES", a_res + 2 * tt - 1 + half)],
                       incs=[("aWO", 1)])
                st("vector", lambda e, tt=tt: e.tensor_add(
                    s_sb[:, 0:512], ps[0][:], xres[tt % 2][:, 0:512]),
                   waits=[("aWO", a_wo + 2 * tt + 1), ("sXR", a_x + 16 * (tt + 1))],
                   incs=[("aRES", 1)])
                st("vector", lambda e, tt=tt: e.tensor_add(
                    s_sb[:, 512:1024], ps[1][:], xres[tt % 2][:, 512:1024]),
                   waits=[("aWO", a_wo + 2 * tt + 2)], incs=[("aRES", 1)])
                if DUMP == "s":
                    st("sync", lambda e, bi=bi, tt=tt: e.dma_start(
                        out=d_x1f[bi, tt * 128:(tt + 1) * 128, :], in_=s_sb[:]),
                       waits=[("aRES", w.c("aRES"))], incs=[("sXS", 16)])
                st("vector", lambda e: e.reduce_sum(colA[:, 0:1], s_sb[:], AXX),
                   waits=[("sXS", w.c("sXS"))] if DUMP == "s" else [], ss=True)
                st("vector", lambda e: e.tensor_scalar_mul(colA[:, 1:2], colA[:, 0:1], -1.0 / D), ss=True)
                st("vector", lambda e: e.tensor_scalar_add(xc_sb[:], s_sb[:], colA[:, 1:2]),
                   waits=[("aSQ", w.c("aSQ"))], incs=[("aXC", 1)], ss=True)
                st("scalar", lambda e: e.activation(
                    (s_sb if DUMP != "s" else x1f_sb)[:], xc_sb[:], AF.Square,
                    accum_out=colA[:, 2:3]),
                   waits=[("aXC", w.c("aXC"))], incs=[("aSQ", 1)])
                st("vector", lambda e: e.tensor_scalar(colA[:, 3:4], colA[:, 2:3],
                                                       1.0 / D, EPS, op0=ALU.mult, op1=ALU.add),
                   waits=[("aSQ", w.c("aSQ"))], incs=[("aVR", 1)], ss=True)
                st("scalar", lambda e: e.activation(colA[:, 4:5], colA[:, 3:4], AF.Sqrt),
                   waits=[("aVR", w.c("aVR"))], incs=[("aSD", 1)])
                st("vector", lambda e: e.reciprocal(colA[:, 5:6], colA[:, 4:5]),
                   waits=[("aSD", w.c("aSD"))], ss=True)
                st("vector", lambda e: e.tensor_scalar_mul(x1f_sb[:], xc_sb[:], colA[:, 5:6]),
                   waits=[("sX1", w.c("sX1"))], ss=True)
                st("vector", lambda e: e.tensor_copy(x1b_sb[:], x1f_sb[:]),
                   waits=[("aXT", w.c("aXT"))], incs=[("aX1", 1)], ss=True)
                if DUMP != "s":
                    st("sync", lambda e, bi=bi, tt=tt: e.dma_start(
                        out=d_x1f[bi, tt * 128:(tt + 1) * 128, :], in_=x1f_sb[:]),
                       waits=[("aX1", w.c("aX1"))], incs=[("sX1", 16)])
                if DUMP != "attn":
                    st("sync", lambda e, bi=bi, tt=tt: e.dma_start(
                        out=d_x1b[bi, tt * 128:(tt + 1) * 128, :], in_=x1b_sb[:]),
                       incs=[("sX1", 16)])
                def trx1(e):
                    ins = None
                    for c in range(8):
                        ins = e.transpose(pst[0][:, c * 128:(c + 1) * 128],
                                          x1b_sb[:, c * 128:(c + 1) * 128], ident_bf[:])
                    return ins
                st("tensor", trx1,
                   waits=[("aX1", w.c("aX1")), ("aXTE", w.c("aXTE"))], incs=[("aXT", 1)])
                st("vector", lambda e: e.tensor_copy(
                    x1T_sb.rearrange("p a b -> p (a b)"), pst[0][:]),
                   waits=[("aXT", w.c("aXT")), ("sX1T", w.c("sX1T")), ("aLG", w.c("aLG"))],
                   incs=[("aXTE", 1)])
                st("sync", lambda e, bi=bi, tt=tt: e.dma_start(
                    out=d_x1T[bi].rearrange("c p t -> p c t")[:, :, tt * 128:(tt + 1) * 128],
                    in_=x1T_sb[:]),
                   waits=[("aXTE", w.c("aXTE"))], incs=[("sX1T", 16)])
                def mmlg(e):
                    ins = None
                    for c in range(8):
                        ins = e.matmul(ps[2][:, 0:65], x1T_sb[:, c, :], gcat[:, c, :],
                                       start=(c == 0), stop=(c == 7))
                    return ins
                st("tensor", mmlg,
                   waits=[("aXTE", w.c("aXTE")), ("aRT2", w.c("aRT2"))], incs=[("aLG", 1)])
                st("vector", lambda e: e.reduce_max(colA[:, 6:7], ps[2][:, 0:64], AXX),
                   waits=[("aLG", w.c("aLG"))], ss=True)
                st("vector", lambda e: e.tensor_scalar_mul(colA[:, 7:8], colA[:, 6:7], -1.0),
                   incs=[("aRT1", 1)], ss=True)
                st("scalar", lambda e: e.activation(prob_sb[:], ps[2][:, 0:64], AF.Exp,
                                                    bias=colA[:, 7:8]),
                   waits=[("aRT1", w.c("aRT1")), ("aRTO", w.c("aRTO"))])
                st("scalar", lambda e, gt=gt: e.activation(
                    g_all[:, gt:gt + 1], ps[2][:, 64:65], AF.Sigmoid),
                   incs=[("aRT2", 1)])
                st("vector", lambda e: e.max(out=w8raw[:], in_=prob_sb[:]),
                   waits=[("aRT2", w.c("aRT2"))], ss=True)
                st("vector", lambda e, gt=gt: e.max_index(
                    out=o8u_all[:, gt * 8:(gt + 1) * 8], in_max=w8raw[:], in_values=prob_sb[:]), ss=True)
                st("vector", lambda e: e.reduce_sum(colA[:, 0:1], w8raw[:], AXX), ss=True)
                st("vector", lambda e: e.reciprocal(colA[:, 1:2], colA[:, 0:1]), ss=True)
                st("vector", lambda e, gt=gt: e.tensor_scalar_mul(
                    w_all[:, gt * 8:(gt + 1) * 8], w8raw[:], colA[:, 1:2]), ss=True)
                st("vector", lambda e, gt=gt: e.tensor_copy(
                    idxf_all[:, gt * 8:(gt + 1) * 8], o8u_all[:, gt * 8:(gt + 1) * 8]), ss=True)
                for j in range(8):
                    if j == 0:
                        st("vector", lambda e, gt=gt: e.tensor_single_scalar(
                            O_bf[:], iota[:], idxf_all[:, gt * 8:gt * 8 + 1], ALU.is_equal), ss=True)
                    else:
                        st("vector", lambda e, gt=gt, j=j: e.scalar_tensor_tensor(
                            out=O_bf[:], in0=iota[:],
                            scalar=idxf_all[:, gt * 8 + j:gt * 8 + j + 1],
                            in1=O_bf[:], op0=ALU.is_equal, op1=ALU.add),
                           incs=[("aRTO", 1)] if j == 7 else [], ss=True)
                st("tensor", lambda e: e.matmul(ps[3][:, 0:64], tri_bf[:], O_bf[:],
                                                start=True, stop=True),
                   waits=[("aRTO", w.c("aRTO")), ("aRT3", w.c("aRT3"))])
                st("tensor", lambda e: e.matmul(ps[4][0:1, 0:64],
                                                ones_col[:], O_bf[:],
                                                start=True, stop=True),
                   incs=[("aPO", 1)])
                st("vector", lambda e, gt=gt: e.tensor_copy(
                    po_all[:, gt * 64:(gt + 1) * 64], ps[3][:, 0:64]),
                   waits=[("aPO", w.c("aPO"))], ss=True)
                cur = (tt % 2) * 64
                nxt = ((tt + 1) % 2) * 64
                st("sync", lambda e, gt=gt, cur=cur: e.dma_start(
                    out=d_tb[gt].unsqueeze(0), in_=run2[0:1, cur:cur + 64]),
                   waits=[("aRUN", w.c("aRUN"))], incs=[("sTB", 16)])
                st("vector", lambda e, cur=cur, nxt=nxt: e.tensor_add(
                    run2[0:1, nxt:nxt + 64], run2[0:1, cur:cur + 64], ps[4][0:1, 0:64]),
                   waits=[("sTB", w.c("sTB"))], incs=[("aRUN", 1)], ss=True)
            st("vector", lambda e, bi=bi: e.tensor_copy(
                cnt_sb[0:1, bi * 64:(bi + 1) * 64], run2[0:1, 0:64]),
               waits=[("aRUN", w.c("aRUN"))], incs=[("aCNT", 1)], ss=True)
            if STOP_AFTER == "batch0" and bi == 0:
                st("sync", waits=[("sX1", w.c("sX1")), ("sX1T", w.c("sX1T"))])
                return

        if not p2:
            st("sync", lambda e: e.dma_start(out=d_cin.flatten().unsqueeze(0), in_=cnt_sb[:]),
               waits=[("aCNT", w.c("aCNT"))], incs=[("sCI", 16)])
            if part == "full":
                st("gpsimd", lambda e: e.collective_compute(
                    "AllGather", ALU.bypass, replica_groups=rg, ins=[d_cin[:]], outs=[d_cout[:]]),
                   waits=[("sCI", w.c("sCI"))], incs=[("sCC", 16)])

        if STOP_AFTER == "A":
            st("sync", waits=[("sX1", w.c("sX1")), ("sCI", w.c("sCI"))])
            return
        w.barrier("a5")

        def a5_emit():
            for fh in range(2):
                st("gpsimd", lambda e, fh=fh: e.dma_start(
                    out=wsgu_sb[:, :, fh * 2048:(fh + 1) * 2048],
                    in_=wsgu.rearrange("(c p) f -> p c f", p=128)[:, :, fh * 2048:(fh + 1) * 2048]),
                   incs=[("s5w", 16)])
            st("gpsimd", lambda e: e.dma_start(
                out=sdw_sb[:], in_=sdw.rearrange("(c p) f -> p c f", p=128)),
               incs=[("s5w", 16)])
            if STOP_AFTER == "a5w":
                st("gpsimd", waits=[("s5w", w.c("s5w"))])
                return
            for ck in range(16 if STOP_AFTER != "a5c0" else 1):
                cbi, cc = divmod(ck, 4)
                st("sync", lambda e, cbi=cbi, cc=cc, ck=ck: e.dma_start(
                    out=x1c[ck % 2][:],
                    in_=d_x1T[cbi].rearrange("c p t -> p c t")[:, :, cc * 512:(cc + 1) * 512]),
                   waits=[("a5gu", max(0, (ck - 1) * 32)), ("sX1T", w.c("sX1T"))],
                   incs=[("s5x", 16)])
                for fp in range(16):
                    pg = ps[0 + 2 * (fp % 2)]
                    pu = ps[1 + 2 * (fp % 2)]
                    def mmgu(e, fp=fp, ck=ck, pg=pg, pu=pu):
                        ins = None
                        for c in range(8):
                            ins = e.matmul(pg[:], wsgu_sb[:, c, fp * 128:(fp + 1) * 128],
                                           x1c[ck % 2][:, c, :], start=(c == 0), stop=(c == 7))
                        for c in range(8):
                            ins = e.matmul(pu[:], wsgu_sb[:, c, 2048 + fp * 128:2048 + (fp + 1) * 128],
                                           x1c[ck % 2][:, c, :], start=(c == 0), stop=(c == 7))
                        return ins
                    st("tensor", mmgu,
                       waits=[("s5w", 48), ("s5x", w.c("s5x")),
                              ("a5ac", max(0, w.c("a5ac") - 1))],
                       incs=[("a5gu", 1)])
                    st("scalar", lambda e, pg=pg: e.activation(sg5[:], pg[:], AF.Silu),
                       waits=[("a5gu", w.c("a5gu")), ("a5ac", max(0, w.c("a5ac")))],
                       incs=[("a5sg", 1)])
                    st("vector", lambda e, fp=fp, pu=pu: e.tensor_tensor(
                        act5[:, fp * 512:(fp + 1) * 512], sg5[:], pu[:], ALU.mult),
                       waits=[("a5sg", w.c("a5sg"))]
                             + ([("a5dn", w.c("a5dn"))] if fp == 0 else []),
                       incs=[("a5ac", 1)])
                for s4 in range(4):
                    gt = ck * 4 + s4
                    def mmdn(e, s4=s4):
                        ins = None
                        for half in range(2):
                            for ic in range(16):
                                ins = e.matmul(ps[4 + half][:],
                                               act5[:, ic * 512 + s4 * 128: ic * 512 + s4 * 128 + 128],
                                               sdw_sb[:, ic, half * 512:(half + 1) * 512],
                                               start=(ic == 0), stop=(ic == 15))
                        return ins
                    st("tensor", mmdn,
                       waits=[("a5ac", w.c("a5ac")), ("a5ev", w.c("a5ev"))],
                       incs=[("a5dn", 1)])
                    st("vector", lambda e, gt=gt: e.tensor_scalar_mul(
                        sh5[:, 0:512], ps[4][:], g_all[:, gt:gt + 1]),
                       waits=[("a5dn", w.c("a5dn")), ("s5st", w.c("s5st"))])
                    st("vector", lambda e, gt=gt: e.tensor_scalar_mul(
                        sh5[:, 512:1024], ps[5][:], g_all[:, gt:gt + 1]),
                       incs=[("a5ev", 1)])
                    st("sync", lambda e, cbi=cbi, cc=cc, s4=s4: e.dma_start(
                        out=d_shared[cbi, (cc * 4 + s4) * 128:(cc * 4 + s4 + 1) * 128, :],
                        in_=sh5[:]),
                       waits=[("a5ev", w.c("a5ev"))], incs=[("s5st", 16)])

        if not p2:
            a5_emit()
        if STOP_AFTER == "a5" and p1:
            st("sync", waits=[("s5st", w.c("s5st"))])
            return

        if p1:
            st("sync", lambda e: e.dma_start(out=d_idxf[:], in_=idxf_all[:]),
               waits=[("aCNT", w.c("aCNT"))], incs=[("sRT", 16)])
            st("sync", lambda e: e.dma_start(out=d_wv[:], in_=w_all[:]), incs=[("sRT", 16)])
            st("sync", lambda e: e.dma_start(out=d_po[:], in_=po_all[:]), incs=[("sRT", 16)])
            st("sync", waits=[("sRT", 48), ("sCI", 16), ("sTB", w.c("sTB")),
                              ("sX1", w.c("sX1")), ("s5st", w.c("s5st"))])
            return

        if PHASES == "A":
            for cbi in range(BPC):
                for tt in range(NT):
                    st("sync", lambda e, cbi=cbi, tt=tt: e.dma_start(
                        out=out4[cbi, tt * 128:(tt + 1) * 128, :], in_=x1f_sb[:]),
                       waits=[("s5st", w.c("s5st"))], incs=[("sOUT", 16)])
            st("sync", waits=[("sOUT", w.c("sOUT"))], incs=[("sFIN", 1)])
            return

        w.barrier("b1")

        st("sync", lambda e: e.dma_start(out=tb32[:], in_=c_tribase[:]), incs=[("sB0", 16)])
        st("sync", lambda e: e.dma_start(out=ccsb[:], in_=d_cout[:]),
           waits=([("sCC", 16)] if part == "full" else []), incs=[("sB0", 16)])
        st("tensor", lambda e: e.matmul(ps[2][0:4, 0:64], tb32[:], ccsb[:],
                                        start=True, stop=True),
           waits=[("sB0", w.c("sB0"))], incs=[("b0a", 1)])
        st("vector", lambda e: e.tensor_copy(bases4[:], ps[2][0:4, 0:64]),
           waits=[("b0a", w.c("b0a"))], incs=[("b0b", 1)], ss=True)
        st("sync", lambda e: e.dma_start(out=d_base[:], in_=bases4[:]),
           waits=[("b0b", w.c("b0b"))], incs=[("sB1", 16)])
        st("sync", lambda e: e.dma_start(
            out=base_row[:], in_=d_base.flatten().unsqueeze(0)),
           waits=[("sB1", w.c("sB1"))], incs=[("sB2l", 16)])

        for gt in range(NTC):
            cbi, tt = divmod(gt, NT)
            st("sync", lambda e, gt=gt: e.dma_start(
                out=rbB[:], in_=d_tb[gt].unsqueeze(0)),
               waits=[("b1bc", w.c("b1bc"))], incs=[("sTBr", 16)])
            st("vector", lambda e, cbi=cbi: e.tensor_add(
                rbB[:], rbB[:], base_row[0:1, cbi * 64:(cbi + 1) * 64]),
               waits=[("sTBr", w.c("sTBr")), ("sB2l", 16)], incs=[("b1rb", 1)], ss=True)
            st("tensor", lambda e: e.matmul(ps[2][:, 0:64], ones_f[:], rbB[:],
                                            start=True, stop=True),
               waits=[("b1rb", w.c("b1rb")), ("b1po", w.c("b1po"))], incs=[("b1bc", 1)])
            st("vector", lambda e, gt=gt: e.scalar_tensor_tensor(
                out=POg[:], in0=po_all[:, gt * 64:(gt + 1) * 64], scalar=1.0,
                in1=ps[2][:, 0:64], op0=ALU.mult, op1=ALU.add),
               waits=[("b1bc", w.c("b1bc"))], incs=[("b1po", 1)], ss=True)
            for j in range(8):
                st("vector", lambda e, gt=gt, j=j: e.scalar_tensor_tensor(
                    out=selB[:], in0=iota[:],
                    scalar=idxf_all[:, gt * 8 + j:gt * 8 + j + 1],
                    in1=POg[:], op0=ALU.is_equal, op1=ALU.mult,
                    accum_out=posg8[:, j:j + 1]), ss=(j == 7))
            st("vector", lambda e, gt=gt: e.tensor_scalar(
                o8f[:], idxf_all[:, gt * 8:(gt + 1) * 8], 512.0, None, op0=ALU.mult), ss=True)
            st("vector", lambda e: e.tensor_add(o8f[:], o8f[:], posg8[:]), ss=True)
            st("vector", lambda e: e.tensor_single_scalar(m8[:], posg8[:], 512.0, ALU.is_ge), ss=True)
            st("vector", lambda e: e.scalar_tensor_tensor(
                out=o8f[:], in0=m8[:], scalar=BIGOFF, in1=o8f[:],
                op0=ALU.mult, op1=ALU.add), ss=True)
            st("vector", lambda e, gt=gt: e.tensor_copy(
                o8u_all[:, gt * 8:(gt + 1) * 8], o8f[:]), ss=True)
            st("vector", lambda e: e.tensor_single_scalar(m8[:], posg8[:], 512.0, ALU.is_lt), ss=True)
            st("vector", lambda e, gt=gt: e.tensor_mul(
                wm_all[:, gt * 8:(gt + 1) * 8], w_all[:, gt * 8:(gt + 1) * 8], m8[:]),
               incs=[("b1of", 1)], ss=True)
            st("sync", lambda e, cbi=cbi, tt=tt, gt=gt: e.dma_start(
                out=x1s[gt % 2][:], in_=d_x1b[cbi, tt * 128:(tt + 1) * 128, :]),
               waits=[("b2sc", max(0, (gt - 1) * 8 * 16)), ("sX1", w.c("sX1"))],
               incs=[("sB2x", 16)])
            for j in range(8):
                st("gpsimd", lambda e, gt=gt, j=j: e.indirect_dma_start(
                    out=d_buf[:],
                    out_offset=IndirectOffsetOnAxis(o8u_all[:, gt * 8 + j:gt * 8 + j + 1], 0),
                    in_=x1s[gt % 2][:], in_offset=None,
                    bounds_check=breg["r"], oob_is_err=False),
                   waits=([("b1of", w.c("b1of")), ("sB2x", w.c("sB2x")), ("sZ", 256 * 16)]
                          if j == 0 else []),
                   incs=[("b2sc", 16)])

        w.barrier("b4")

        scat_tot = w.c("b2sc")
        for e_i in range(E):
            st("gpsimd", lambda en, e_i=e_i: en.dma_start(
                out=wgu[e_i % 2][:], in_=egu[e_i].rearrange("(c p) f -> p c f", p=128)),
               waits=[("b2sc", scat_tot), ("b4gu", max(0, 8 * (e_i - 1)))],
               incs=[("sEW", 16)])
            st("gpsimd", lambda en, e_i=e_i: en.dma_start(
                out=wdw[e_i % 2][:], in_=edw[e_i].rearrange("(c p) f -> p c f", p=128)),
               waits=[("b4dn", max(0, 4 * (e_i - 1)))], incs=[("sEW", 16)])
            st("sync", lambda en, e_i=e_i: en.dma_start(
                out=rows[e_i % 2][:],
                in_=d_buf[e_i * 512:(e_i + 1) * 512, :].rearrange("(s p) d -> p s d", p=128)),
               waits=[("b2sc", scat_tot), ("b4tr", max(0, 4 * (e_i - 1)))],
               incs=[("sER", 16)])
            for s4 in range(4):
                def mtr(en, e_i=e_i, s4=s4):
                    ins = None
                    for c in range(8):
                        ins = en.transpose(pst[s4 % 2][:, c * 128:(c + 1) * 128],
                                           rows[e_i % 2][:, s4, c * 128:(c + 1) * 128],
                                           ident_bf[:])
                    return ins
                st("tensor", mtr,
                   waits=[("sER", w.c("sER")), ("b4te", max(0, w.c("b4te") - 1))],
                   incs=[("b4tr", 1)])
                st("vector", lambda en, e_i=e_i, s4=s4: en.tensor_copy(
                    rowsT[e_i % 2].rearrange("p (c t) -> p c t", c=8)[:, :, s4 * 128:(s4 + 1) * 128],
                    pst[s4 % 2].rearrange("p (c t) -> p c t", c=8)),
                   waits=[("b4tr", w.c("b4tr"))]
                         + ([("b4gu", max(0, 8 * (e_i - 1)))] if s4 == 0 else []),
                   incs=[("b4te", 1)])
            for fp in range(4):
                pg = ps[0 + 2 * (fp % 2)]
                pu = ps[1 + 2 * (fp % 2)]
                def mmge(en, e_i=e_i, fp=fp, pg=pg, pu=pu):
                    ins = None
                    for c in range(8):
                        ins = en.matmul(pg[:], wgu[e_i % 2][:, c, fp * 128:(fp + 1) * 128],
                                        rowsT[e_i % 2][:, c * 512:(c + 1) * 512],
                                        start=(c == 0), stop=(c == 7))
                    for c in range(8):
                        ins = en.matmul(pu[:], wgu[e_i % 2][:, c, 512 + fp * 128:512 + (fp + 1) * 128],
                                        rowsT[e_i % 2][:, c * 512:(c + 1) * 512],
                                        start=(c == 0), stop=(c == 7))
                    return ins
                st("tensor", mmge,
                   waits=[("sEW", w.c("sEW")), ("b4te", w.c("b4te")),
                          ("b4ac", max(0, w.c("b4ac") - 1))],
                   incs=[("b4gu", 1)])
                st("scalar", lambda en, pg=pg: en.activation(sgeB[:], pg[:], AF.Silu),
                   waits=[("b4gu", w.c("b4gu")), ("b4ac", w.c("b4ac"))],
                   incs=[("b4sg", 1)])
                st("vector", lambda en, fp=fp, pu=pu: en.tensor_tensor(
                    acteB[:, fp * 512:(fp + 1) * 512], sgeB[:], pu[:], ALU.mult),
                   waits=[("b4sg", w.c("b4sg"))]
                         + ([("b4dn", w.c("b4dn"))] if fp == 0 else []),
                   incs=[("b4ac", 1)])
            for s4 in range(4):
                def mmde(en, e_i=e_i, s4=s4):
                    ins = None
                    for half in range(2):
                        for ic in range(4):
                            ins = en.matmul(ps[4 + half][:],
                                            acteB[:, ic * 512 + s4 * 128:ic * 512 + s4 * 128 + 128],
                                            wdw[e_i % 2][:, ic, half * 512:(half + 1) * 512],
                                            start=(ic == 0), stop=(ic == 3))
                    return ins
                st("tensor", mmde,
                   waits=[("b4ac", w.c("b4ac")), ("b4ev", w.c("b4ev"))],
                   incs=[("b4dn", 1)])
                st("vector", lambda en: en.tensor_copy(outeB[:, 0:512], ps[4][:]),
                   waits=[("b4dn", w.c("b4dn")), ("sEO", w.c("sEO"))])
                st("vector", lambda en: en.tensor_copy(outeB[:, 512:1024], ps[5][:]),
                   incs=[("b4ev", 1)])
                st("scalar", lambda en, e_i=e_i, s4=s4: en.dma_start(
                    out=d_obuf[e_i * 512 + s4 * 128:e_i * 512 + (s4 + 1) * 128, :],
                    in_=outeB[:]),
                   waits=[("b4ev", w.c("b4ev"))], incs=[("sEO", 16)])

        w.barrier("b5")

        eo_tot = w.c("sEO")
        st("vector", lambda e: e.memset(G8.rearrange("p a b -> p (a b)"), 0.0),
           incs=[("b5gz", 1)])
        for gt in range(NTC):
            cbi, tt = divmod(gt, NT)
            for j in range(8):
                st("gpsimd", lambda e, gt=gt, j=j: e.indirect_dma_start(
                    out=G8[:, j, :], out_offset=None,
                    in_=d_obuf[:],
                    in_offset=IndirectOffsetOnAxis(o8u_all[:, gt * 8 + j:gt * 8 + j + 1], 0),
                    bounds_check=breg["r"], oob_is_err=False),
                   waits=([("sEO", eo_tot), ("b5gz", 1), ("b5cmb", w.c("b5cmb"))]
                          if j == 0 else []),
                   incs=[("b5g", 16)])
            st("sync", lambda e, cbi=cbi, tt=tt: e.dma_start(
                out=xrB[gt % 2][:], in_=d_x1f[cbi, tt * 128:(tt + 1) * 128, :]),
               waits=[("b5ln", max(0, gt - 1))], incs=[("b5x", 16)])
            st("sync", lambda e, cbi=cbi, tt=tt: e.dma_start(
                out=shB[gt % 2][:], in_=d_shared[cbi, tt * 128:(tt + 1) * 128, :]),
               incs=[("b5x", 16)])
            g5 = w.c("b5g")
            st("vector", lambda e, gt=gt: e.tensor_scalar_mul(
                accB[0][:], G8[:, 0, :], wm_all[:, gt * 8:gt * 8 + 1]),
               waits=[("b5g", g5)], ss=True)
            for j in range(1, 8):
                st("vector", lambda e, gt=gt, j=j: e.scalar_tensor_tensor(
                    out=accB[j % 2][:], in0=G8[:, j, :],
                    scalar=wm_all[:, gt * 8 + j:gt * 8 + j + 1],
                    in1=accB[(j - 1) % 2][:], op0=ALU.mult, op1=ALU.add),
                   incs=[("b5cmb", 1)] if j == 7 else [], ss=True)
            st("vector", lambda e, gt=gt: e.tensor_add(accB[1][:], accB[1][:], shB[gt % 2][:]),
               waits=[("b5x", w.c("b5x"))], ss=True)
            st("vector", lambda e, gt=gt: e.tensor_add(sB_ln[:], accB[1][:], xrB[gt % 2][:]),
               waits=[("sOUT", w.c("sOUT"))], ss=True)
            st("vector", lambda e: e.reduce_sum(colB[:, 0:1], sB_ln[:], AXX), ss=True)
            st("vector", lambda e: e.tensor_scalar_mul(colB[:, 1:2], colB[:, 0:1], -1.0 / D), ss=True)
            st("vector", lambda e: e.tensor_scalar_add(xcB[:], sB_ln[:], colB[:, 1:2]),
               waits=[("b5sq", w.c("b5sq"))], incs=[("b5xc", 1)], ss=True)
            st("scalar", lambda e: e.activation(sB_ln[:], xcB[:], AF.Square,
                                                accum_out=colB[:, 2:3]),
               waits=[("b5xc", w.c("b5xc"))], incs=[("b5sq", 1)])
            st("vector", lambda e: e.tensor_scalar(colB[:, 3:4], colB[:, 2:3], 1.0 / D, EPS,
                                                   op0=ALU.mult, op1=ALU.add),
               waits=[("b5sq", w.c("b5sq"))], incs=[("b5vr", 1)], ss=True)
            st("scalar", lambda e: e.activation(colB[:, 4:5], colB[:, 3:4], AF.Sqrt),
               waits=[("b5vr", w.c("b5vr"))], incs=[("b5sd", 1)])
            st("vector", lambda e: e.reciprocal(colB[:, 5:6], colB[:, 4:5]),
               waits=[("b5sd", w.c("b5sd"))], ss=True)
            st("vector", lambda e: e.tensor_scalar_mul(sB_ln[:], xcB[:], colB[:, 5:6]),
               incs=[("b5ln", 1)], ss=True)
            st("sync", lambda e, cbi=cbi, tt=tt: e.dma_start(
                out=out4[cbi, tt * 128:(tt + 1) * 128, :], in_=sB_ln[:]),
               waits=[("b5ln", w.c("b5ln"))], incs=[("sOUT", 16)])
        st("sync", waits=[("sOUT", w.c("sOUT"))], incs=[("sFIN", 1)])

    def make(name):
        def run(eng):
            w.begin(name, eng)
            program()
        return run

    blk.sync(make("sync"))
    blk.tensor(make("tensor"))
    blk.vector(make("vector"))
    blk.scalar(make("scalar"))
    blk.gpsimd(make("gpsimd"))

    stack.close()
    return nc


def kernel(L0_attn_x, L0_attn_Wq, L0_attn_bq, L0_attn_Wk, L0_attn_bk,
           L0_attn_Wv, L0_attn_bv, L0_attn_Wo, L0_attn_bo, L0_attn_rel_bias,
           L0_ffn_gate_w, L0_ffn_shared_gate_w, L0_ffn_shared_gate_up_w,
           L0_ffn_shared_down_w, L0_ffn_expert_gate_up_w, L0_ffn_expert_down_w):
    x = np.ascontiguousarray(np.asarray(L0_attn_x))
    weights = {
        "Wq": L0_attn_Wq, "bq": L0_attn_bq, "Wk": L0_attn_Wk, "bk": L0_attn_bk,
        "Wv": L0_attn_Wv, "bv": L0_attn_bv, "Wo": L0_attn_Wo, "bo": L0_attn_bo,
        "relb": L0_attn_rel_bias, "gate_w": L0_ffn_gate_w, "sgw": L0_ffn_shared_gate_w,
        "wsgu": L0_ffn_shared_gate_up_w, "sdw": L0_ffn_shared_down_w,
        "egu": L0_ffn_expert_gate_up_w, "edw": L0_ffn_expert_down_w,
    }
    weights = {k: np.ascontiguousarray(np.asarray(v)) for k, v in weights.items()}
    cores = list(range(NCORE))
    if PART_MODE == "split":
        P1_IN = {"x4", "Wq", "bq", "Wk", "bk", "Wv", "bv", "Wo", "bo", "relb",
                 "gate_w", "sgw", "wsgu", "sdw",
                 "c_ident_bf", "c_iota", "c_ones_f", "c_ident_f", "c_tri_bf",
                 "c_ones_bf", "c_stairidx", "c_G", "c_ones_col"}
        P2_IN = {"egu", "edw", "c_ident_bf", "c_iota", "c_ones_f", "c_tribase"}
        nc1 = build_nc(part="p1", dbg=DBG)
        in_maps1 = []
        for c in cores:
            batches = [c + 8 * i for i in range(BPC)]
            m = {"x4": np.ascontiguousarray(x[batches])}
            m.update(weights)
            m.update(host_consts(c))
            in_maps1.append({k: v for k, v in m.items() if k in P1_IN})
        res1 = run_bass_kernel_spmd(nc1, in_maps1, core_ids=cores)
        r1 = res1.results
        cnt_all = np.zeros((NB, E), dtype=np.float32)
        for r in range(NCORE):
            for i2 in range(BPC):
                cnt_all[r * 4 + i2] = r1[r]["d_cin"][i2]
        nc2 = build_nc(part="p2", dbg=DBG)
        in_maps2 = []
        for c in cores:
            hc = host_consts(c)
            m = {"egu": weights["egu"], "edw": weights["edw"],
                 "c_ident_bf": hc["c_ident_bf"], "c_iota": hc["c_iota"],
                 "c_ones_f": hc["c_ones_f"], "c_tribase": hc["c_tribase"],
                 "d_cout": cnt_all,
                 "d_idxf": r1[c]["d_idxf"], "d_wv": r1[c]["d_wv"],
                 "d_po": r1[c]["d_po"], "d_tb": r1[c]["d_tb"],
                 "d_x1f": r1[c]["d_x1f"], "d_x1b": r1[c]["d_x1b"],
                 "d_shared": r1[c]["d_shared"]}
            in_maps2.append(m)
        res2 = run_bass_kernel_spmd(nc2, in_maps2, core_ids=cores)
        out = np.empty((NB, S, D), dtype=np.float32)
        for c in cores:
            batches = [c + 8 * i for i in range(BPC)]
            out[batches] = res2.results[c]["out4"]
        return out
    nc = build_nc(dbg=DBG)
    in_maps = []
    for c in cores:
        batches = [c + 8 * i for i in range(BPC)]
        m = {"x4": np.ascontiguousarray(x[batches])}
        m.update(weights)
        m.update(host_consts(c))
        in_maps.append(m)
    res = run_bass_kernel_spmd(nc, in_maps, core_ids=cores)
    out = np.empty((NB, S, D), dtype=np.float32)
    for c in cores:
        batches = [c + 8 * i for i in range(BPC)]
        out[batches] = res.results[c]["out4"]
    return out
